# version 13 (final)
# speedup vs baseline: 1.7814x; 1.0152x over previous
import numpy as np
from contextlib import ExitStack

import concourse.bass as bass
import concourse.mybir as mybir
from concourse.bass_utils import run_bass_kernel_spmd

F32 = mybir.dt.float32
BF16 = mybir.dt.bfloat16
AF = mybir.ActivationFunctionType
ALU = mybir.AluOpType
AX = mybir.AxisListType

D = 1024
DCH = 8
NSEQ = 8192
NB = 4
HEADS = 16
DH = 64
WIN = 256
OWN = 4096
NW = 16
DEPTH = 6
DI = 2730
DIP = 2816
NPAIR = 22
HALOW = 7
PADL = HALOW * WIN + 32
XW = PADL + OWN
DK = 128
TB = 256
XT = 257
INVD = 1.0 / 1024.0
EPS = 1e-5
RSQ128 = float(1.0 / np.sqrt(np.float32(128.0)))


import os as _os
SELF_SYNC = _os.environ.get("BASS_SIM_FP64") == "1"

class Plan:
    def __init__(self):
        self.ops = []
        from collections import defaultdict
        self.cnt = defaultdict(int)
        self.guard = {}

    def emit(self, eng, meth, *args, waits=(), incs=(), **kwargs):
        seen = {}
        for (s, v) in waits:
            if not v:
                continue
            if seen.get(s, 0) < v:
                seen[s] = v
        if eng in ("ve", "ac", "po") and meth not in ("wait_only", "dma_start"):
            if not any(s == eng for (s, n) in incs):
                incs = tuple(incs) + ((eng, 1),)
            if SELF_SYNC and self.cnt[eng] > 0 and seen.get(eng, 0) < self.cnt[eng]:
                seen[eng] = self.cnt[eng]
        ri = []
        for (s, n) in incs:
            self.cnt[s] += n
            ri.append((s, n))
        self.ops.append((eng, tuple(seen.items()), meth, args, kwargs, tuple(ri)))

    def pe_mark(self):
        for idx in range(len(self.ops) - 1, -1, -1):
            if self.ops[idx][0] == "pe":
                e, w, m, a, k, i = self.ops[idx]
                if any(s == "pe" for (s, n) in i):
                    return self.cnt["pe"]
                self.cnt["pe"] += 1
                self.ops[idx] = (e, w, m, a, k, i + (("pe", 1),))
                return self.cnt["pe"]
        raise RuntimeError("no pe op to mark")

    def protect(self, key, sem, val):
        if val:
            self.guard.setdefault(key, []).append((sem, val))

    def need(self, key):
        return tuple(self.guard.pop(key, ()))


def dma(pl, eng, sem, out, in_, waits=()):
    pl.emit(eng, "dma_start", waits=waits, incs=((sem, 16),), out=out, in_=in_)


def a_first(layer):
    return layer - 6


def build_program(depth_emit=DEPTH):
    nc = bass.Bass("TRN2", target_bir_lowering=False)
    pl = Plan()

    def din(name, shape):
        return nc.dram_tensor(name, list(shape), F32, kind="ExternalInput")

    jext = din("jext", (D, XW))
    joth = din("joth", (D, OWN))
    tabs = din("tabs", (6, 128, XW))
    maskw0 = din("maskw0", (128, 2 * 384))
    maskre = din("maskre", (128, 384))
    edgev = din("edgev", (128, 1))
    identF = din("identF", (128, 128))
    onesC = din("onesC", (128, 1))
    onesR = din("onesR", (1, 128))
    hselT = din("hselT", (16, DCH * 128))
    hselB = din("hselB", (128, DCH * 16))
    ehsel = din("ehsel", (128, 16 * 16))
    wqkv = din("wqkv", (DEPTH, DCH, 128, 3072))
    wo = din("wo", (DEPTH, DCH, 128, 1024))
    w1 = din("w1", (DEPTH, DCH, 128, 2 * DIP))
    w2 = din("w2", (DEPTH, NPAIR, 128, 1024))
    wqT = din("wqT", (DCH, 128, DK))
    wkT = din("wkT", (DCH, 128, DK))
    qsv = din("qsv", (128, DEPTH))
    ksv = din("ksv", (128, DEPTH))
    lng = din("lng", (128, DEPTH * DCH))
    lnb = din("lnb", (128, DEPTH * DCH))
    b2v = din("b2v", (128, DEPTH * DCH))
    b1a = din("b1a", (128, DEPTH * NPAIR))
    b1g = din("b1g", (128, DEPTH * NPAIR))
    gamv = din("gamv", (128, DCH))
    betv = din("betv", (128, DCH))

    xscr = [nc.dram_tensor("xscr%d" % i, [D, XW], F32, kind="Internal") for i in (0, 1)]
    acs = nc.dram_tensor("acs", [D, OWN], F32, kind="Internal")
    outT = nc.dram_tensor("outT", [D, OWN], F32, kind="ExternalOutput")

    jext_c = jext.ap().rearrange("(c p) t -> p c t", p=128)
    joth_c = joth.ap().rearrange("(c p) t -> p c t", p=128)
    xscr_c = [x.ap().rearrange("(c p) t -> p c t", p=128) for x in xscr]
    xscr_b = [x.ap().rearrange("(c p) t -> c p t", p=128) for x in xscr]
    acs_c = acs.ap().rearrange("(c p) t -> p c t", p=128)
    outT_c = outT.ap().rearrange("(c p) t -> p c t", p=128)
    def xa3(buf, base, width):
        return buf[:, base:base + DCH * width].rearrange("p (c t) -> p c t", t=width)

    wqkv_a = wqkv.ap()
    wo_a = wo.ap()
    w1_a = w1.ap()
    w2_a = w2.ap()
    tabs_a = tabs.ap()

    with ExitStack() as top:
        E = top.enter_context
        sems = {}
        for s in ("ls", "lx0", "lx1", "lm", "lj", "st", "wC", "wW", "wT",
                  "wX", "pe", "ve", "ac", "po"):
            sems[s] = E(nc.semaphore("s_" + s))

        ident_sb = E(nc.sbuf_tensor("ident_sb", [128, 128], BF16))
        onesB = E(nc.sbuf_tensor("onesB", [128, 1], BF16))
        onesRF = E(nc.sbuf_tensor("onesRF", [1, 128], F32))
        hselT_sb = E(nc.sbuf_tensor("hselT_sb", [16, DCH * 128], BF16))
        hselB_sb = E(nc.sbuf_tensor("hselB_sb", [128, DCH * 16], BF16))
        ehsel_sb = E(nc.sbuf_tensor("ehsel_sb", [128, 256], BF16))
        maskw0_sb = E(nc.sbuf_tensor("maskw0_sb", [128, 2 * 384], BF16))
        maskre_sb = E(nc.sbuf_tensor("maskre_sb", [128, 384], BF16))
        edge_sb = E(nc.sbuf_tensor("edge_sb", [128, 1], F32))
        cEPS = E(nc.sbuf_tensor("cEPS", [128, 1], F32))
        cTINY = E(nc.sbuf_tensor("cTINY", [128, 1], F32))
        qs_sb = E(nc.sbuf_tensor("qs_sb", [128, DEPTH], F32))
        ks_sb = E(nc.sbuf_tensor("ks_sb", [128, DEPTH], F32))
        lng_sb = E(nc.sbuf_tensor("lng_sb", [128, DEPTH * DCH], F32))
        lnb_sb = E(nc.sbuf_tensor("lnb_sb", [128, DEPTH * DCH], F32))
        b2_sb = E(nc.sbuf_tensor("b2_sb", [128, DEPTH * DCH], F32))
        b1a_sb = E(nc.sbuf_tensor("b1a_sb", [128, DEPTH * NPAIR], F32))
        b1g_sb = E(nc.sbuf_tensor("b1g_sb", [128, DEPTH * NPAIR], F32))
        gam_sb = E(nc.sbuf_tensor("gam_sb", [128, DCH], F32))
        bet_sb = E(nc.sbuf_tensor("bet_sb", [128, DCH], F32))
        wqT_sb = E(nc.sbuf_tensor("wqT_sb", [128, DCH * DK], BF16))
        wkT_sb = E(nc.sbuf_tensor("wkT_sb", [128, DCH * DK], BF16))

        xa = E(nc.sbuf_tensor("xa", [128, 2 * DCH * XT], F32))
        xb = E(nc.sbuf_tensor("xb", [128, 2 * DCH * XT], BF16))
        tabsb = E(nc.sbuf_tensor("tabsb", [128, 6 * 256], BF16))
        sq_s = E(nc.sbuf_tensor("sq_s", [128, 4 * 256], BF16))
        xo_s = E(nc.sbuf_tensor("xo_s", [128, DCH * 256], F32))
        musr_s = E(nc.sbuf_tensor("musr_s", [1, 512], F32))
        mu_s = E(nc.sbuf_tensor("mu_s", [128, 256], F32))
        rstd_s = E(nc.sbuf_tensor("rstd_s", [128, 256], F32))
        nb_s = E(nc.sbuf_tensor("nb_s", [128, 256], F32))
        var_s = E(nc.sbuf_tensor("var_s", [128, 256], F32))
        u_s = E(nc.sbuf_tensor("u_s", [128, 2 * 256], F32))
        xnf_s = E(nc.sbuf_tensor("xnf_s", [128, DCH * 256], BF16))
        g_s = E(nc.sbuf_tensor("g_s", [128, 2 * 256], BF16))
        gg_s = E(nc.sbuf_tensor("gg_s", [128, 2 * 256], BF16))
        w2_sb = E(nc.sbuf_tensor("w2_sb", [128, NPAIR * 1024], BF16))

        for (dst, src) in ((onesRF, onesR), (edge_sb, edgev), (qs_sb, qsv),
                           (ks_sb, ksv), (lng_sb, lng), (lnb_sb, lnb),
                           (b2_sb, b2v), (b1a_sb, b1a), (b1g_sb, b1g),
                           (gam_sb, gamv), (bet_sb, betv)):
            dma(pl, "sy", "ls", dst[:, :], src.ap())
        for (dst, src) in ((ident_sb, identF), (onesB, onesC),
                           (hselT_sb, hselT), (hselB_sb, hselB),
                           (ehsel_sb, ehsel), (maskw0_sb, maskw0),
                           (maskre_sb, maskre)):
            dma(pl, "po", "wC", dst[:, :], src.ap())
        for (dst, src) in ((wqT_sb, wqT), (wkT_sb, wkT)):
            sa = src.ap()
            for c in range(DCH):
                dma(pl, "po", "wC", dst[:, c * DK:(c + 1) * DK], sa[c])
        pl.emit("ve", "memset", incs=(("ve", 1),), ap=cEPS[:, :], constant=EPS)
        pl.emit("ve", "memset", incs=(("ve", 1),), ap=cTINY[:, :], constant=1e-38)
        pl.emit("ve", "memset", incs=(("ve", 1),), ap=xo_s[:, 0:512], constant=0.0)
        _vz = pl.cnt["ve"]
        for c in range(DCH):
            dma(pl, "sy", "st", xscr_b[1][c][:, 0:512], xo_s[:, 0:512],
                waits=(("ve", _vz),) if c == 0 else ())
        pl.protect(("xo",), "st", pl.cnt["st"])
        pl.protect(("p0xo",), "st", pl.cnt["st"])
        PRE_LD = pl.cnt["ls"]
        PRE_WT = pl.cnt["wC"]

        with ExitStack() as s0:
            E0 = s0.enter_context
            ps_st = E0(nc.psum_tensor("p0_st", [1, 512], F32))
            ps_b = E0(nc.psum_tensor("p0_b", [128, 512], F32))
            ps_m = E0(nc.psum_tensor("p0_m", [128, 1024], F32))
            ps_w = E0(nc.psum_tensor("p0_w", [128, 256], F32))
            ps_t = E0(nc.psum_tensor("p0_t", [128, 1024], BF16))
            ps_k = E0(nc.psum_tensor("p0_k", [128, 256], F32))
            ps_kt = E0(nc.psum_tensor("p0_kt", [128, 128], BF16))
            mjn_s = E0(nc.sbuf_tensor("mjn_s", [128, DCH * 256], F32))
            msb_s = E0(nc.sbuf_tensor("msb_s", [128, DK * DCH], BF16))
            ktr_s = E0(nc.sbuf_tensor("ktr_s", [128, 128], BF16))
            xnt_s = E0(nc.sbuf_tensor("xnt_s", [128, 1024], BF16))
            kfm_s = E0(nc.sbuf_tensor("kfm_s", [128, 128], BF16))
            qb_s = E0(nc.sbuf_tensor("qb_s", [128, 256], BF16))
            m_started = [False]

            def p0_ln(src_c, col0, T, store_col, want_xnf, ld_waits=()):
                wx = pl.need(("p0xa",))
                dma(pl, "sy", "lx0", xa3(xa, 0, T),
                    src_c[:, :, col0:col0 + T],
                    waits=(wx + ld_waits + (("ls", PRE_LD),)))
                vld = pl.cnt["lx0"]
                wb = pl.need(("p0xb",))
                pl.emit("ac", "copy", waits=(("lx0", vld),) + wb, incs=(("ac", 1),),
                        out=xb[:, 0:DCH * T], in_=xa[:, 0:DCH * T])
                vcast = pl.cnt["ac"]
                wst = pl.need(("p0st",))
                for c in range(DCH):
                    pl.emit("pe", "matmul",
                            waits=((("ac", vcast), ("wC", PRE_WT)) + wst) if c == 0 else (),
                            out=ps_st[0:1, 0:T], lhsT=onesB[:, :],
                            rhs=xb[:, c * T:c * T + T],
                            start=(c == 0), stop=(c == DCH - 1), skip_group_check=True)
                for c in range(DCH):
                    s = c % 2
                    wsq = pl.need(("p0sq", s))
                    pl.emit("ac", "activation", waits=wsq, incs=(("ac", 1),),
                            out=sq_s[:, s * 256:s * 256 + T],
                            in_=xb[:, c * T:c * T + T], func=AF.Square,
                            scale=1.0)
                    vsq = pl.cnt["ac"]
                    pl.emit("pe", "matmul", waits=(("ac", vsq),),
                            out=ps_st[0:1, T:2 * T], lhsT=onesB[:, :],
                            rhs=sq_s[:, s * 256:s * 256 + T],
                            start=(c == 0), stop=(c == DCH - 1), skip_group_check=True)
                    pl.pe_mark()
                    pl.protect(("p0sq", s), "pe", pl.cnt["pe"])
                pl.pe_mark()
                vpe = pl.cnt["pe"]
                wmu = pl.need(("p0mu",))
                pl.emit("ac", "copy", waits=(("pe", vpe),) + wmu, incs=(("ac", 1),),
                        out=musr_s[0:1, 0:2 * T], in_=ps_st[0:1, 0:2 * T])
                vmu = pl.cnt["ac"]
                pl.protect(("p0st",), "ac", vmu)
                wpb = pl.need(("p0pb",))
                pl.emit("pe", "matmul", waits=(("ac", vmu),) + wpb,
                        out=ps_b[:, 0:T], lhsT=onesRF[0:1, :], rhs=musr_s[0:1, 0:T],
                        start=True, stop=True, skip_group_check=True)
                pl.emit("pe", "matmul",
                        out=ps_b[:, T:2 * T], lhsT=onesRF[0:1, :],
                        rhs=musr_s[0:1, T:2 * T], start=True, stop=True,
                        skip_group_check=True)
                pl.pe_mark()
                vpb = pl.cnt["pe"]
                pl.protect(("p0mu",), "pe", vpb)
                pl.emit("ve", "tensor_scalar", waits=(("pe", vpb),), incs=(("ve", 1),),
                        out=mu_s[:, 0:T], in0=ps_b[:, 0:T], scalar1=INVD,
                        scalar2=None, op0=ALU.mult)
                pl.emit("ve", "tensor_tensor", incs=(("ve", 1),),
                        out=var_s[:, 0:T], in0=mu_s[:, 0:T], in1=mu_s[:, 0:T],
                        op=ALU.mult)
                pl.emit("ve", "scalar_tensor_tensor", incs=(("ve", 1),),
                        out=var_s[:, 0:T], in0=ps_b[:, T:2 * T], scalar=INVD,
                        in1=var_s[:, 0:T], op0=ALU.mult, op1=ALU.subtract)
                vvv = pl.cnt["ve"]
                pl.protect(("p0pb",), "ve", vvv)
                pl.emit("ac", "activation", waits=(("ve", vvv),), incs=(("ac", 1),),
                        out=rstd_s[:, 0:T], in_=var_s[:, 0:T], func=AF.Ln, bias=cEPS[:, 0:1],
                        scale=1.0)
                pl.emit("ac", "activation", incs=(("ac", 1),),
                        out=rstd_s[:, 0:T], in_=rstd_s[:, 0:T], func=AF.Exp,
                        scale=-0.5)
                vrs = pl.cnt["ac"]
                pl.emit("ve", "tensor_tensor", waits=(("ac", vrs),), incs=(("ve", 1),),
                        out=nb_s[:, 0:T], in0=mu_s[:, 0:T], in1=rstd_s[:, 0:T],
                        op=ALU.mult)
                vnb = pl.cnt["ve"]
                wxo = pl.need(("p0xo",))
                for c in range(DCH):
                    eng = "ve" if c % 2 == 0 else "po"
                    pl.emit(eng, "tensor_tensor",
                            waits=(("ve", vnb),) + (wxo if c < 2 else ()) +
                                  ((("wC", PRE_WT),) if c < 2 else ()),
                            incs=((eng, 1),),
                            out=u_s[:, (c % 2) * 256:(c % 2) * 256 + T],
                            in0=xa[:, c * T:c * T + T], in1=rstd_s[:, 0:T],
                            op=ALU.mult)
                    pl.emit(eng, "tensor_tensor", incs=((eng, 1),),
                            out=xo_s[:, c * 256:c * 256 + T],
                            in0=u_s[:, (c % 2) * 256:(c % 2) * 256 + T],
                            in1=nb_s[:, 0:T], op=ALU.subtract)
                vve, vpo = pl.cnt["ve"], pl.cnt["po"]
                pl.protect(("p0xa",), "ve", vve)
                pl.protect(("p0xa",), "po", vpo)
                pl.protect(("p0xa",), "ac", vcast)
                pl.protect(("p0xb",), "pe", vpe)
                if store_col is not None:
                    dma(pl, "sy", "st",
                        xscr_c[0][:, :, store_col:store_col + T],
                        xa3(xo_s, 0, 256)[:, :, 0:T],
                        waits=(("ve", vve), ("po", vpo)))
                    pl.protect(("p0xo",), "st", pl.cnt["st"])
                vxnf = 0
                if want_xnf:
                    wxn = pl.need(("p0xnf",))
                    pl.emit("ac", "copy", waits=(("ve", vve), ("po", vpo)) + wxn,
                            incs=(("ac", 1),),
                            out=xnf_s[:, 0:DCH * 256], in_=xo_s[:, 0:DCH * 256])
                    vxnf = pl.cnt["ac"]
                    pl.protect(("p0xo",), "ac", vxnf)
                else:
                    pl.protect(("p0xo",), "ve", vve)
                    pl.protect(("p0xo",), "po", vpo)
                return vxnf

            def p0_mgroups(vxnf, widths):
                for (off, wdt) in widths:
                    wk = pl.need(("p0k",))
                    for c in range(DCH):
                        pl.emit("pe", "matmul", waits=((("ac", vxnf),) + wk) if c == 0 else (),
                                out=ps_k[:, 0:wdt],
                                lhsT=wkT_sb[:, c * DK:(c + 1) * DK],
                                rhs=xnf_s[:, c * 256 + off:c * 256 + off + wdt],
                                start=(c == 0), stop=(c == DCH - 1),
                                skip_group_check=True)
                    pl.pe_mark()
                    vk = pl.cnt["pe"]
                    wkf = pl.need(("p0kfm",))
                    pl.emit("ac", "copy", waits=(("pe", vk),) + wkf, incs=(("ac", 1),),
                            out=kfm_s[:, 0:wdt], in_=ps_k[:, 0:wdt])
                    vkc = pl.cnt["ac"]
                    pl.protect(("p0k",), "ac", vkc)
                    wkt = pl.need(("p0kt",))
                    pl.emit("pe", "transpose", waits=(("ac", vkc),) + wkt,
                            out=ps_kt[0:wdt, 0:128], in_=kfm_s[:, 0:wdt],
                            identity=ident_sb[:, :])
                    pl.pe_mark()
                    vkt = pl.cnt["pe"]
                    pl.protect(("p0kfm",), "pe", vkt)
                    wktr = pl.need(("p0ktr",))
                    pl.emit("ve", "tensor_copy", waits=(("pe", vkt),) + wktr,
                            incs=(("ve", 1),),
                            out=ktr_s[0:wdt, 0:128], in_=ps_kt[0:wdt, 0:128])
                    vktr = pl.cnt["ve"]
                    pl.protect(("p0kt",), "ve", vktr)
                    wxt = pl.need(("p0xt",))
                    for c in range(DCH):
                        pl.emit("pe", "transpose", waits=wxt if c == 0 else (),
                                out=ps_t[0:wdt, c * 128:c * 128 + 128],
                                in_=xnf_s[:, c * 256 + off:c * 256 + off + wdt],
                                identity=ident_sb[:, :])
                    pl.pe_mark()
                    vxt = pl.cnt["pe"]
                    wxnt = pl.need(("p0xnt",))
                    pl.emit("ve", "tensor_copy", waits=(("pe", vxt),) + wxnt,
                            incs=(("ve", 1),),
                            out=xnt_s[0:wdt, 0:1024], in_=ps_t[0:wdt, 0:1024])
                    vxn2 = pl.cnt["ve"]
                    pl.protect(("p0xt",), "ve", vxn2)
                    st0 = not m_started[0]
                    m_started[0] = True
                    pl.emit("pe", "matmul", waits=(("ve", max(vktr, vxn2)),),
                            out=ps_m[:, 0:512], lhsT=ktr_s[0:wdt, 0:128],
                            rhs=xnt_s[0:wdt, 0:512],
                            start=st0, stop=False, skip_group_check=True)
                    pl.emit("pe", "matmul",
                            out=ps_m[:, 512:1024], lhsT=ktr_s[0:wdt, 0:128],
                            rhs=xnt_s[0:wdt, 512:1024],
                            start=st0, stop=False, skip_group_check=True)
                    pl.pe_mark()
                    pl.protect(("p0ktr",), "pe", pl.cnt["pe"])
                    pl.protect(("p0xnt",), "pe", pl.cnt["pe"])
                    pl.protect(("p0xnf",), "pe", pl.cnt["pe"])

            for t in range(23):
                col = t * 256
                owned = col >= PADL
                vx = p0_ln(jext_c, col, 256, col, owned)
                if owned:
                    p0_mgroups(vx, ((0, 128), (128, 128)))
            p0_ln(jext_c, 23 * 256, 32, 23 * 256, False)
            vx = p0_ln(jext_c, PADL, 224, None, True)
            p0_mgroups(vx, ((0, 128), (128, 96)))
            vx = p0_ln(jext_c, XW - 32, 32, None, True)
            p0_mgroups(vx, ((0, 32),))
            for t in range(OWN // 256):
                vx = p0_ln(joth_c, t * 256, 256, None, True)
                p0_mgroups(vx, ((0, 128), (128, 128)))

            pl.pe_mark()
            vmm = pl.cnt["pe"]
            pl.emit("ve", "tensor_scalar", waits=(("pe", vmm),), incs=(("ve", 1),),
                    out=msb_s[:, :], in0=ps_m[:, 0:1024], scalar1=RSQ128,
                    scalar2=None, op0=ALU.mult)
            vmsb = pl.cnt["ve"]
            P0_ST = pl.cnt["st"]

            for t in range(OWN // 256):
                col = PADL + t * 256
                wx = pl.need(("p0xa",))
                dma(pl, "sy", "lx0", xa3(xa, 0, 256),
                    xscr_c[0][:, :, col:col + 256],
                    waits=wx + (("st", P0_ST),))
                vld = pl.cnt["lx0"]
                wb = pl.need(("p0xb",))
                pl.emit("ac", "copy", waits=(("lx0", vld),) + wb, incs=(("ac", 1),),
                        out=xb[:, 0:DCH * 256], in_=xa[:, 0:DCH * 256])
                vcast = pl.cnt["ac"]
                wq = pl.need(("p0k",))
                for c in range(DCH):
                    pl.emit("pe", "matmul",
                            waits=((("ve", vmsb), ("ac", vcast)) + wq) if c == 0 else (),
                            out=ps_k[:, 0:256],
                            lhsT=wqT_sb[:, c * DK:(c + 1) * DK],
                            rhs=xb[:, c * 256:c * 256 + 256],
                            start=(c == 0), stop=(c == DCH - 1), skip_group_check=True)
                pl.pe_mark()
                vq = pl.cnt["pe"]
                wqb = pl.need(("p0qb",))
                pl.emit("ac", "copy", waits=(("pe", vq),) + wqb, incs=(("ac", 1),),
                        out=qb_s[:, 0:256], in_=ps_k[:, 0:256])
                vqb = pl.cnt["ac"]
                pl.protect(("p0k",), "ac", vqb)
                for c in range(DCH):
                    wjn = pl.need(("p0w",))
                    pl.emit("pe", "matmul", waits=(("ac", vqb),) + wjn,
                            out=ps_w[:, 0:256], lhsT=msb_s[:, c * 128:(c + 1) * 128],
                            rhs=qb_s[:, 0:256], start=True, stop=True,
                            skip_group_check=True)
                    pl.pe_mark()
                    vjn = pl.cnt["pe"]
                    wmj = pl.need(("p0mjn", c))
                    pl.emit("ve", "tensor_tensor", waits=(("pe", vjn),) + wmj,
                            incs=(("ve", 1),),
                            out=mjn_s[:, c * 256:(c + 1) * 256], in0=ps_w[:, 0:256],
                            in1=xa[:, c * 256:c * 256 + 256], op=ALU.add)
                    pl.protect(("p0w",), "ve", pl.cnt["ve"])
                vjns = pl.cnt["ve"]
                pl.protect(("p0qb",), "pe", pl.cnt["pe"])
                wxn = pl.need(("p0xnf",))
                pl.emit("ac", "copy", waits=(("ve", vjns),) + wxn, incs=(("ac", 1),),
                        out=xnf_s[:, 0:DCH * 256], in_=mjn_s[:, 0:DCH * 256])
                vjb = pl.cnt["ac"]
                wst = pl.need(("p0st",))
                for c in range(DCH):
                    pl.emit("pe", "matmul", waits=((("ac", vjb),) + wst) if c == 0 else (),
                            out=ps_st[0:1, 0:256], lhsT=onesB[:, :],
                            rhs=xnf_s[:, c * 256:c * 256 + 256],
                            start=(c == 0), stop=(c == DCH - 1), skip_group_check=True)
                for c in range(DCH):
                    s = c % 2
                    wsq = pl.need(("p0sq", s))
                    pl.emit("ac", "activation", waits=wsq, incs=(("ac", 1),),
                            out=sq_s[:, s * 256:s * 256 + 256],
                            in_=xnf_s[:, c * 256:c * 256 + 256],
                            func=AF.Square, scale=1.0)
                    vsq = pl.cnt["ac"]
                    pl.emit("pe", "matmul", waits=(("ac", vsq),),
                            out=ps_st[0:1, 256:512], lhsT=onesB[:, :],
                            rhs=sq_s[:, s * 256:s * 256 + 256],
                            start=(c == 0), stop=(c == DCH - 1), skip_group_check=True)
                    pl.pe_mark()
                    pl.protect(("p0sq", s), "pe", pl.cnt["pe"])
                pl.pe_mark()
                vst = pl.cnt["pe"]
                wmu = pl.need(("p0mu",))
                pl.emit("ac", "copy", waits=(("pe", vst),) + wmu, incs=(("ac", 1),),
                        out=musr_s[0:1, 0:512], in_=ps_st[0:1, 0:512])
                vmu = pl.cnt["ac"]
                pl.protect(("p0st",), "ac", vmu)
                wpb = pl.need(("p0pb",))
                pl.emit("pe", "matmul", waits=(("ac", vmu),) + wpb,
                        out=ps_b[:, 0:256], lhsT=onesRF[0:1, :],
                        rhs=musr_s[0:1, 0:256], start=True, stop=True,
                        skip_group_check=True)
                pl.emit("pe", "matmul",
                        out=ps_b[:, 256:512], lhsT=onesRF[0:1, :],
                        rhs=musr_s[0:1, 256:512], start=True, stop=True,
                        skip_group_check=True)
                pl.pe_mark()
                vpb = pl.cnt["pe"]
                pl.protect(("p0mu",), "pe", vpb)
                pl.emit("ve", "tensor_scalar", waits=(("pe", vpb),), incs=(("ve", 1),),
                        out=mu_s[:, 0:256], in0=ps_b[:, 0:256], scalar1=INVD,
                        scalar2=None, op0=ALU.mult)
                pl.emit("ve", "tensor_tensor", incs=(("ve", 1),),
                        out=var_s[:, 0:256], in0=mu_s[:, 0:256], in1=mu_s[:, 0:256],
                        op=ALU.mult)
                pl.emit("ve", "scalar_tensor_tensor", incs=(("ve", 1),),
                        out=var_s[:, 0:256], in0=ps_b[:, 256:512], scalar=INVD,
                        in1=var_s[:, 0:256], op0=ALU.mult, op1=ALU.subtract)
                vvv = pl.cnt["ve"]
                pl.protect(("p0pb",), "ve", vvv)
                pl.emit("ac", "activation", waits=(("ve", vvv),), incs=(("ac", 1),),
                        out=rstd_s[:, 0:256], in_=var_s[:, 0:256], func=AF.Ln,
                        bias=cEPS[:, 0:1], scale=1.0)
                pl.emit("ac", "activation", incs=(("ac", 1),),
                        out=rstd_s[:, 0:256], in_=rstd_s[:, 0:256], func=AF.Exp,
                        scale=-0.5)
                vrs = pl.cnt["ac"]
                pl.emit("ve", "tensor_tensor", waits=(("ac", vrs),), incs=(("ve", 1),),
                        out=nb_s[:, 0:256], in0=mu_s[:, 0:256], in1=rstd_s[:, 0:256],
                        op=ALU.mult)
                vnb = pl.cnt["ve"]
                for c in range(DCH):
                    eng = "ve" if c % 2 == 0 else "po"
                    wu = pl.need(("p0u", c % 2))
                    pl.emit(eng, "tensor_tensor", waits=(("ve", vnb),) + wu,
                            incs=((eng, 1),),
                            out=u_s[:, (c % 2) * 256:(c % 2) * 256 + 256],
                            in0=mjn_s[:, c * 256:(c + 1) * 256], in1=rstd_s[:, 0:256],
                            op=ALU.mult)
                    pl.emit(eng, "tensor_tensor", incs=((eng, 1),),
                            out=u_s[:, (c % 2) * 256:(c % 2) * 256 + 256],
                            in0=u_s[:, (c % 2) * 256:(c % 2) * 256 + 256],
                            in1=nb_s[:, 0:256], op=ALU.subtract)
                    vu = pl.cnt[eng]
                    wxo = pl.need(("p0xo",))
                    pl.emit("ac", "activation", waits=((eng, vu),) + wxo,
                            incs=(("ac", 1),),
                            out=xo_s[:, c * 256:c * 256 + 256],
                            in_=u_s[:, (c % 2) * 256:(c % 2) * 256 + 256],
                            func=AF.Identity, bias=bet_sb[:, c:c + 1],
                            scale=gam_sb[:, c:c + 1])
                    pl.protect(("p0u", c % 2), "ac", pl.cnt["ac"])
                vg = pl.cnt["ac"]
                dma(pl, "sy", "st", acs_c[:, :, t * 256:(t + 1) * 256],
                    xa3(xo_s, 0, 256), waits=(("ac", vg),))
                pl.protect(("p0xo",), "st", pl.cnt["st"])
                pl.protect(("p0xa",), "ve", vjns)
                pl.protect(("p0xa",), "ac", vcast)
                pl.protect(("p0xb",), "pe", vq)
                pl.protect(("p0xnf",), "pe", vst)
                pl.protect(("p0mjn", 0), "ve", 0)
                for c in range(DCH):
                    pl.protect(("p0mjn", c), "ac", vjb)
                    pl.protect(("p0mjn", c), "ve", pl.cnt["ve"])
            ACS_ST = pl.cnt["st"]
            pl.protect(("acs",), "st", ACS_ST)
            pl.protect(("xscr0",), "st", P0_ST)
            pl.protect(("xa", 0), "ve", pl.cnt["ve"])
            pl.protect(("xa", 0), "ac", pl.cnt["ac"])
            pl.protect(("xb", 0), "pe", pl.cnt["pe"])
            pl.protect(("xo",), "st", pl.cnt["st"])
            pl.protect(("xnfB",), "pe", pl.cnt["pe"])

        for _k in (("warena",), ("w2",)):
            pl.protect(_k, "pe", pl.cnt["pe"])
            pl.protect(_k, "ve", pl.cnt["ve"])
            pl.protect(_k, "ac", pl.cnt["ac"])
            pl.protect(_k, "po", pl.cnt["po"])
        XSCR_ST = {0: P0_ST, 1: 0}

        for layer in range(depth_emit):
            a0 = a_first(layer)
            with ExitStack() as sA:
                EA = sA.enter_context
                wqkv_sb = EA(nc.sbuf_tensor("wqkv_sb_%d" % layer, [128, DCH * 3072], BF16))
                wo_sb = EA(nc.sbuf_tensor("wo_sb_%d" % layer, [128, DCH * 1024], BF16))
                qraw = EA(nc.sbuf_tensor("qraw_%d" % layer, [128, 2 * DCH * 256], BF16))
                qhi = EA(nc.sbuf_tensor("qhi_%d" % layer, [128, 2 * DCH * 256], BF16))
                kraw = EA(nc.sbuf_tensor("kraw_%d" % layer, [128, DCH * 256], BF16))
                krot_s = EA(nc.sbuf_tensor("krot_s_%d" % layer, [128, 2 * 256], BF16))
                kt_s = EA(nc.sbuf_tensor("kt_s_%d" % layer, [128, 2 * 256], BF16))
                rsn_s = EA(nc.sbuf_tensor("rsn_s_%d" % layer, [16, 512], F32))
                rsnb_s = EA(nc.sbuf_tensor("rsnb_s_%d" % layer, [16, 512], BF16))
                attn_s = EA(nc.sbuf_tensor("attn_s_%d" % layer, [128, 2 * 384], BF16))
                attnT_s = EA(nc.sbuf_tensor("attnT_s_%d" % layer, [128, 2 * 384], BF16))
                m8_s = EA(nc.sbuf_tensor("m8_s_%d" % layer, [128, 4], F32))
                r_s = EA(nc.sbuf_tensor("r_s_%d" % layer, [16, 256], F32))
                rb16_s = EA(nc.sbuf_tensor("rb16_s_%d" % layer, [16, 256], BF16))
                rbc_s = EA(nc.sbuf_tensor("rbc_s_%d" % layer, [128, 2 * 256], F32))
                woin_s = EA(nc.sbuf_tensor("woin_s_%d" % layer, [128, DCH * 256], BF16))
                KHI0 = 8192

                def khiS(p0, p1, c0, c1):
                    return w2_sb[p0:p1, KHI0 + c0:KHI0 + c1]
                KLO0 = 4096

                def kloS(p0, p1, c0, c1):
                    return w2_sb[p0:p1, KLO0 + c0:KLO0 + c1]
                vring = w2_sb
                ps_qk = EA(nc.psum_tensor("psA_qk_%d" % layer, [128, 512], F32))
                ps_sc = EA(nc.psum_tensor("psA_sc_%d" % layer, [128, 1024], F32))
                ps_tr = EA(nc.psum_tensor("psA_tr_%d" % layer, [128, 384], BF16))
                ps_ao = EA(nc.psum_tensor("psA_ao_%d" % layer, [128, 512], F32))
                ps_dl = EA(nc.psum_tensor("psA_dl_%d" % layer, [128, 1024], F32))
                ps_sm = EA(nc.psum_tensor("psA_sm_%d" % layer, [128, 512], F32))
                ps_v = ps_dl

                LD_GATE_A = (("lx0", pl.cnt["lx0"]), ("lx1", pl.cnt["lx1"]))
                wload = pl.need(("warena",))
                for c in range(DCH):
                    dma(pl, "po", "wW", wqkv_sb[:, c * 3072:(c + 1) * 3072],
                        wqkv_a[layer, c], waits=wload if c == 0 else ())
                for c in range(DCH):
                    dma(pl, "po", "wW", wo_sb[:, c * 1024:(c + 1) * 1024],
                        wo_a[layer, c])
                VWT_A = pl.cnt["wW"]
                if layer == 0:
                    pl.emit("ve", "memset", waits=(("wW", VWT_A),), incs=(("ve", 1),),
                            ap=khiS(0, 128, 0, 4096), constant=0.0)
                    pl.emit("ve", "memset", incs=(("ve", 1),), ap=kloS(0, 128, 0, 4096), constant=0.0)
                    pl.emit("ve", "memset", incs=(("ve", 1),), ap=vring[:, :], constant=0.0)

                st = {}

                def xcolm1(w):
                    return PADL + 256 * w - 1

                def xb_sh(par, kc, width=256):
                    off = (par * DCH + kc) * XT + (1 if kc < 4 else 0)
                    return xb[:, off:off + width]

                def xa_un(par, kc, width=256):
                    off = (par * DCH + kc) * XT + 1
                    return xa[:, off:off + width]

                def plan_load(w, par):
                    lx = "lx%d" % par
                    wx = pl.need(("xa", par))
                    gate = (("st", XSCR_ST[(2 * layer) % 2]),)
                    dma(pl, "sy", lx, xa3(xa, par * DCH * XT, XT),
                        xscr_c[(2 * layer) % 2][:, :, xcolm1(w):xcolm1(w) + XT],
                        waits=wx + gate)
                    st[("ldx", w)] = (lx, pl.cnt[lx])
                    wt_ = pl.need(("tabs",))
                    for ti in range(6):
                        dma(pl, "po", "wT",
                            tabsb[:, ti * 256:(ti + 1) * 256],
                            tabs_a[ti][:, xcolm1(w) + 1:xcolm1(w) + 1 + 256],
                            waits=wt_ if ti == 0 else ())
                    st[("ldt", w)] = pl.cnt["wT"]

                def plan_cast(w, par):
                    vedge = 0
                    if w == 0:
                        for c in range(4, DCH):
                            pl.emit("ve", "tensor_scalar",
                                    waits=(st[("ldx", w)],) if c == 4 else (),
                                    incs=(("ve", 1),),
                                    out=xa[:, (par * DCH + c) * XT:(par * DCH + c) * XT + 1],
                                    in0=xa[:, (par * DCH + c) * XT:(par * DCH + c) * XT + 1],
                                    scalar1=edge_sb[:, 0:1], scalar2=None, op0=ALU.mult)
                        vedge = pl.cnt["ve"]
                    wb = pl.need(("xb", par))
                    pl.emit("ac", "copy",
                            waits=wb + (st[("ldx", w)], ("ve", vedge)),
                            incs=(("ac", 1),),
                            out=xb[:, par * DCH * XT:(par + 1) * DCH * XT],
                            in_=xa[:, par * DCH * XT:(par + 1) * DCH * XT])
                    st[("cast", w)] = pl.cnt["ac"]

                def plan_qkproj(w, par, do_q):
                    vcast = st[("cast", w)]
                    kinds = [("k", m) for m in range(DCH)]
                    if do_q:
                        kinds += [("q", m) for m in range(DCH)]
                    nr_first = {0: True, 1: True}
                    LAGQ = 2
                    marks = {}

                    def qk_ap(half):
                        if half == 0:
                            return ps_qk[:, 0:256]
                        return ps_ao[:, 256:512]

                    def emit_g(idx):
                        kind, m = kinds[idx]
                        half = idx % 2
                        slot = idx % 4
                        wq = pl.need(("psqk", half))
                        colbase = (0 if kind == "q" else 1024) + m * 128
                        for kc in range(DCH):
                            w0 = ()
                            if kc == 0:
                                w0 = wq + (("ac", vcast), ("wW", VWT_A))
                            pl.emit("pe", "matmul", waits=w0,
                                    out=qk_ap(half),
                                    lhsT=wqkv_sb[:, kc * 3072 + colbase:kc * 3072 + colbase + 128],
                                    rhs=xb_sh(par, kc),
                                    start=(kc == 0), stop=(kc == DCH - 1),
                                    skip_group_check=True)
                        pl.pe_mark()
                        vmm = pl.cnt["pe"]
                        if kind == "q":
                            dst, doff, key = qraw, (par * DCH + m) * 256, ("qraw", par, m)
                        else:
                            dst, doff, key = kraw, m * 256, ("kraw", m)
                        wdst = pl.need(key)
                        pl.emit("ac", "copy", waits=(("pe", vmm),) + wdst,
                                incs=(("ac", 1),),
                                out=dst[:, doff:doff + 256],
                                in_=qk_ap(half))
                        wsq = pl.need(("sq", slot))
                        pl.emit("ac", "activation", waits=wsq, incs=(("ac", 1),),
                                out=sq_s[:, slot * 256:(slot + 1) * 256],
                                in_=qk_ap(half),
                                func=AF.Square, scale=1.0)
                        vsq = pl.cnt["ac"]
                        pl.protect(("psqk", half), "ac", vsq)
                        marks[idx] = vsq

                    def emit_n(idx):
                        kind, m = kinds[idx]
                        slot = idx % 4
                        qk = 0 if kind == "q" else 1
                        wnr = pl.need(("psnr", qk)) if nr_first[qk] else ()
                        pl.emit("pe", "matmul", waits=(("ac", marks[idx]),) + wnr,
                                out=ps_sm[0:16, qk * 256:qk * 256 + 256],
                                lhsT=hselB_sb[:, m * 16:(m + 1) * 16],
                                rhs=sq_s[:, slot * 256:(slot + 1) * 256],
                                start=(nr_first[0] and nr_first[1]),
                                stop=(m == DCH - 1), skip_group_check=True)
                        nr_first[qk] = False
                        pl.pe_mark()
                        pl.protect(("sq", slot), "pe", pl.cnt["pe"])

                    for idx in range(len(kinds) + LAGQ):
                        if idx < len(kinds):
                            emit_g(idx)
                        if idx >= LAGQ:
                            emit_n(idx - LAGQ)
                    st[("nrm", w)] = pl.cnt["pe"]

                def plan_chain(w, do_q):
                    lo = 0 if do_q else 256
                    wv = pl.need(("rsn",))
                    pl.emit("ve", "tensor_scalar",
                            waits=(("pe", st[("nrm", w)]),) + wv, incs=(("ve", 1),),
                            out=rsn_s[:, lo:512], in0=ps_sm[0:16, lo:512],
                            scalar1=1e-24, scalar2=None, op0=ALU.max)
                    vv = pl.cnt["ve"]
                    pl.protect(("psnr", 0), "ve", vv)
                    pl.protect(("psnr", 1), "ve", vv)
                    pl.emit("ac", "activation", waits=(("ve", vv),), incs=(("ac", 1),),
                            out=rsn_s[:, lo:512], in_=rsn_s[:, lo:512], func=AF.Ln,
                            scale=1.0)
                    pl.emit("ac", "activation", incs=(("ac", 1),),
                            out=rsnb_s[:, lo:512], in_=rsn_s[:, lo:512], func=AF.Exp,
                            scale=-0.5)
                    st[("rsn", w)] = pl.cnt["ac"]
                    pl.protect(("rsn",), "ac", pl.cnt["ac"])

                def plan_epi(w, par, do_q, ring_war):
                    vrsn = st[("rsn", w)]
                    slot = w % 2
                    tb0 = 0
                    vtab = st[("ldt", w)]
                    first_ve = True
                    first_po = True
                    for m in range(DCH):
                        half = m % 2
                        wrb = pl.need(("psrb",))
                        pl.emit("pe", "matmul", waits=(("ac", vrsn),) + wrb,
                                out=ps_sc[:, 768:1024],
                                lhsT=hselT_sb[0:16, m * 128:(m + 1) * 128],
                                rhs=rsnb_s[0:16, 256:512],
                                start=True, stop=True, skip_group_check=True)
                        pl.pe_mark()
                        vrb = pl.cnt["pe"]
                        wrbc = pl.need(("rbc", half))
                        pl.emit("ac", "copy", waits=(("pe", vrb),) + wrbc,
                                incs=(("ac", 1),),
                                out=rbc_s[:, half * 256:half * 256 + 256],
                                in_=ps_sc[:, 768:1024])
                        vrbe = pl.cnt["ac"]
                        pl.protect(("psrb",), "ac", vrbe)
                        evw = (("ac", vrbe),)
                        if first_ve:
                            evw = evw + ring_war
                            first_ve = False
                        pl.emit("ve", "scalar_tensor_tensor", waits=evw,
                                incs=(("ve", 1),),
                                out=kraw[:, m * 256:(m + 1) * 256],
                                in0=kraw[:, m * 256:(m + 1) * 256],
                                scalar=ks_sb[:, layer:layer + 1],
                                in1=rbc_s[:, half * 256:half * 256 + 256],
                                op0=ALU.mult, op1=ALU.mult)
                        vkh = pl.cnt["ve"]
                        pl.protect(("rbc", half), "ve", vkh)
                        wkr = pl.need(("krot", half))
                        pow0 = (("ve", vkh),) + wkr
                        if first_po:
                            pow0 = pow0 + ring_war
                            first_po = False
                        for i2, (o, i) in enumerate(((0, 32), (32, 0), (64, 96), (96, 64))):
                            pl.emit("po", "tensor_copy",
                                    waits=pow0 if i2 == 0 else (),
                                    incs=(("po", 1),),
                                    out=krot_s[o:o + 32, half * 256:half * 256 + 256],
                                    in_=kraw[i:i + 32, m * 256:(m + 1) * 256])
                        vkr = pl.cnt["po"]
                        wkt = pl.need(("kt", half))
                        pl.emit("ve", "tensor_tensor",
                                waits=(("po", vkr), ("wT", vtab)) + wkt,
                                incs=(("ve", 1),),
                                out=kt_s[:, half * 256:half * 256 + 256],
                                in0=kraw[:, m * 256:(m + 1) * 256],
                                in1=tabsb[:, tb0 + 2 * 256:tb0 + 3 * 256],
                                op=ALU.mult)
                        pl.emit("ve", "tensor_tensor", incs=(("ve", 1),),
                                out=khiS(0, 128, m * 512 + slot * 256, m * 512 + (slot + 1) * 256),
                                in0=krot_s[:, half * 256:half * 256 + 256],
                                in1=tabsb[:, tb0 + 3 * 256:tb0 + 4 * 256],
                                op=ALU.mult)
                        pl.emit("ve", "tensor_tensor", incs=(("ve", 1),),
                                out=khiS(0, 128, m * 512 + slot * 256, m * 512 + (slot + 1) * 256),
                                in0=khiS(0, 128, m * 512 + slot * 256, m * 512 + (slot + 1) * 256),
                                in1=kt_s[:, half * 256:half * 256 + 256],
                                op=ALU.add)
                        vkhi = pl.cnt["ve"]
                        pl.protect(("kt", half), "ve", vkhi)
                        pl.emit("po", "tensor_tensor", waits=(("wT", vtab),),
                                incs=(("po", 1),),
                                out=kloS(0, 128, m * 512 + slot * 256, m * 512 + (slot + 1) * 256),
                                in0=kraw[:, m * 256:(m + 1) * 256],
                                in1=tabsb[:, tb0 + 4 * 256:tb0 + 5 * 256],
                                op=ALU.mult)
                        pl.emit("po", "tensor_tensor", waits=(("ve", vkhi),),
                                incs=(("po", 1),),
                                out=krot_s[:, half * 256:half * 256 + 256],
                                in0=krot_s[:, half * 256:half * 256 + 256],
                                in1=tabsb[:, tb0 + 5 * 256:tb0 + 6 * 256],
                                op=ALU.mult)
                        pl.emit("po", "tensor_tensor", incs=(("po", 1),),
                                out=kloS(0, 128, m * 512 + slot * 256, m * 512 + (slot + 1) * 256),
                                in0=kloS(0, 128, m * 512 + slot * 256, m * 512 + (slot + 1) * 256),
                                in1=krot_s[:, half * 256:half * 256 + 256],
                                op=ALU.add)
                        vklo = pl.cnt["po"]
                        pl.protect(("krot", half), "po", vklo)
                        pl.protect(("kraw", m), "ve", vkhi)
                        pl.protect(("kraw", m), "po", vklo)
                        st[("ring", w, m)] = (vkhi, vklo)
                        if do_q:
                            pl.emit("pe", "matmul", waits=(("ac", vrbe),),
                                    out=ps_sc[:, 768:1024],
                                    lhsT=hselT_sb[0:16, m * 128:(m + 1) * 128],
                                    rhs=rsnb_s[0:16, 0:256],
                                    start=True, stop=True, skip_group_check=True)
                            pl.pe_mark()
                            vrbq = pl.cnt["pe"]
                            wrbc2 = pl.need(("rbc", half))
                            pl.emit("ac", "copy", waits=(("pe", vrbq),) + wrbc2,
                                    incs=(("ac", 1),),
                                    out=rbc_s[:, half * 256:half * 256 + 256],
                                    in_=ps_sc[:, 768:1024])
                            vrbqe = pl.cnt["ac"]
                            pl.protect(("psrb",), "ac", vrbqe)
                            qoff = (par * DCH + m) * 256
                            wqh = pl.need(("qhi", par, m))
                            pl.emit("ve", "scalar_tensor_tensor",
                                    waits=(("ac", vrbqe),), incs=(("ve", 1),),
                                    out=qraw[:, qoff:qoff + 256],
                                    in0=qraw[:, qoff:qoff + 256],
                                    scalar=qs_sb[:, layer:layer + 1],
                                    in1=rbc_s[:, half * 256:half * 256 + 256],
                                    op0=ALU.mult, op1=ALU.mult)
                            pl.emit("ve", "tensor_tensor",
                                    waits=(("wT", vtab),) + wqh, incs=(("ve", 1),),
                                    out=qhi[:, qoff:qoff + 256],
                                    in0=qraw[:, qoff:qoff + 256],
                                    in1=tabsb[:, tb0 + 0 * 256:tb0 + 1 * 256],
                                    op=ALU.mult)
                            pl.emit("ve", "tensor_tensor", incs=(("ve", 1),),
                                    out=qraw[:, qoff:qoff + 256],
                                    in0=qraw[:, qoff:qoff + 256],
                                    in1=tabsb[:, tb0 + 1 * 256:tb0 + 2 * 256],
                                    op=ALU.mult)
                            st[("qrdy", w, m)] = pl.cnt["ve"]
                            pl.protect(("rbc", half), "ve", pl.cnt["ve"])
                    st[("epi", w)] = (pl.cnt["ve"], pl.cnt["po"])
                    pl.protect(("tabs",), "ve", pl.cnt["ve"])
                    pl.protect(("tabs",), "po", pl.cnt["po"])

                def plan_vproj(w, par):
                    slot = w % 2
                    vcast = st[("cast", w)]
                    wdl_v = pl.need(("psdlA",))
                    for g in range(2):
                        for eh in range(2):
                            vslot = (g * 2 + eh) % 2
                            wv = pl.need(("psv", vslot)) + (wdl_v if g == 0 else ())
                            for kc in range(DCH):
                                pl.emit("pe", "matmul",
                                        waits=(wv + (("ac", vcast),)) if kc == 0 else (),
                                        out=ps_dl[:, vslot * 512:vslot * 512 + 512],
                                        lhsT=xb_sh(par, kc)[:, g * 128:g * 128 + 128],
                                        rhs=wqkv_sb[:, kc * 3072 + 2048 + eh * 512:kc * 3072 + 2048 + (eh + 1) * 512],
                                        start=(kc == 0), stop=(kc == DCH - 1),
                                        skip_group_check=True)
                            pl.pe_mark()
                            vmm = pl.cnt["pe"]
                            wvr = pl.need(("vring", slot))
                            pl.emit("ve", "tensor_copy", waits=(("pe", vmm),) + wvr,
                                    incs=(("ve", 1),),
                                    out=vring[:, slot * 2048 + g * 1024 + eh * 512:
                                              slot * 2048 + g * 1024 + (eh + 1) * 512],
                                    in_=ps_dl[:, vslot * 512:vslot * 512 + 512])
                            pl.protect(("psv", vslot), "ve", pl.cnt["ve"])
                    pl.protect(("psdlA",), "ve", pl.cnt["ve"])
                    st[("vrdy", w)] = pl.cnt["ve"]

                def plan_attn(w, par):
                    slot, slotp = w % 2, (w - 1) % 2
                    vvr = st[("vrdy", w)]
                    chains = [(h, s2) for h in range(HEADS) for s2 in (0, 1)]
                    NC_ = len(chains)

                    def emit_A(X):
                        h, s2 = chains[X]
                        m = h // 2
                        b64 = (h % 2) * 64
                        vq = st[("qrdy", w, m)]
                        vkhi, vklo = st[("ring", w, m)]
                        qoff = (par * DCH + m) * 256
                        wps = pl.need(("pssc", s2))
                        n1 = 256 - 128 * s2
                        r1 = m * 512 + slotp * 256 + 128 * s2
                        r2 = m * 512 + slot * 256
                        n2 = 128 + 128 * s2
                        sc0 = s2 * 512
                        pl.emit("pe", "matmul",
                                waits=(("ve", max(vq, vkhi)), ("po", vklo)) + wps,
                                out=ps_sc[:, sc0:sc0 + n1],
                                lhsT=qhi[b64:b64 + 64, qoff + 128 * s2:qoff + 128 * s2 + 128],
                                rhs=khiS(b64, b64 + 64, r1, r1 + n1),
                                start=True, stop=False, skip_group_check=True)
                        pl.emit("pe", "matmul",
                                out=ps_sc[:, sc0 + n1:sc0 + 384],
                                lhsT=qhi[b64:b64 + 64, qoff + 128 * s2:qoff + 128 * s2 + 128],
                                rhs=khiS(b64, b64 + 64, r2, r2 + n2),
                                start=(sc0 // 512 != (sc0 + n1) // 512), stop=False,
                                skip_group_check=True)
                        pl.emit("pe", "matmul",
                                out=ps_sc[:, sc0:sc0 + n1],
                                lhsT=qraw[b64:b64 + 64, qoff + 128 * s2:qoff + 128 * s2 + 128],
                                rhs=kloS(b64, b64 + 64, r1, r1 + n1),
                                start=False, stop=True, skip_group_check=True)
                        pl.emit("pe", "matmul",
                                out=ps_sc[:, sc0 + n1:sc0 + 384],
                                lhsT=qraw[b64:b64 + 64, qoff + 128 * s2:qoff + 128 * s2 + 128],
                                rhs=kloS(b64, b64 + 64, r2, r2 + n2),
                                start=False, stop=True, skip_group_check=True)
                        pl.pe_mark()
                        vsc = pl.cnt["pe"]
                        wm8 = pl.need(("m8", s2))
                        pl.emit("ve", "reduce_max", waits=(("pe", vsc),) + wm8,
                                incs=(("ve", 1),),
                                out=m8_s[:, s2:s2 + 1], in_=ps_sc[:, sc0:sc0 + 384],
                                axis=AX.X, negate=True)
                        vmx = pl.cnt["ve"]
                        wat = pl.need(("attn", s2))
                        pl.emit("ac", "activation", waits=(("ve", vmx),) + wat,
                                incs=(("ac", 1),),
                                out=attn_s[:, s2 * 384:(s2 + 1) * 384],
                                in_=ps_sc[:, sc0:sc0 + 384], func=AF.Exp,
                                bias=m8_s[:, s2:s2 + 1], scale=1.0)
                        vex = pl.cnt["ac"]
                        pl.protect(("pssc", s2), "ac", vex)
                        pl.protect(("m8", s2), "ac", vex)
                        st[("exp", w, X)] = vex

                    def emit_B(X):
                        h, s2 = chains[X]
                        vex = st[("exp", w, X)]
                        wtr = pl.need(("pstr",))
                        for k3 in range(3):
                            pl.emit("pe", "transpose",
                                    waits=((("ac", vex),) + wtr) if k3 == 0 else (),
                                    out=ps_tr[:, k3 * 128:(k3 + 1) * 128],
                                    in_=attn_s[:, s2 * 384 + k3 * 128:s2 * 384 + (k3 + 1) * 128],
                                    identity=ident_sb[:, :])
                        pl.pe_mark()
                        vtr = pl.cnt["pe"]
                        pl.protect(("attn", s2), "pe", vtr)
                        waT = pl.need(("attnT", s2))
                        pl.emit("ve", "tensor_tensor", waits=(("pe", vtr),) + waT,
                                incs=(("ve", 1),),
                                out=attnT_s[:, s2 * 384:(s2 + 1) * 384],
                                in0=ps_tr[:, 0:384],
                                in1=(maskw0_sb[:, s2 * 384:(s2 + 1) * 384]
                                     if w == 0 else maskre_sb[:, 0:384]),
                                op=ALU.mult)
                        vaT = pl.cnt["ve"]
                        pl.protect(("pstr",), "ve", vaT)
                        st[("aT", w, X)] = vaT

                    def emit_C(X):
                        h, s2 = chains[X]
                        m = h // 2
                        b64 = (h % 2) * 64
                        vaT = st[("aT", w, X)]
                        wss = pl.need(("psss",)) if X == 0 else ()
                        for k3 in range(3):
                            pl.emit("pe", "matmul", waits=(("ve", vaT),) + wss,
                                    out=ps_sm[32:48, 128 * s2:128 * s2 + 128],
                                    lhsT=ehsel_sb[:, h * 16:(h + 1) * 16],
                                    rhs=attnT_s[:, s2 * 384 + k3 * 128:s2 * 384 + (k3 + 1) * 128],
                                    start=(X == 0 and k3 == 0),
                                    stop=(X == NC_ - 1 and k3 == 2),
                                    skip_group_check=True)
                        wao = pl.need(("psao",)) if (h % 2 == 0 and s2 == 0) else ()
                        for k3 in range(3):
                            jrel = 128 * (s2 + k3)
                            q_ = jrel // 256
                            ghalf = (jrel % 256) // 128
                            vslot = (w - 1 + q_) % 2
                            vcols = vslot * 2048 + ghalf * 1024 + h * 64
                            pl.emit("pe", "matmul",
                                    waits=(("ve", max(vaT, vvr)),) + wao,
                                    out=ps_ao[b64:b64 + 64, 128 * s2:128 * s2 + 128],
                                    lhsT=vring[:, vcols:vcols + 64],
                                    rhs=attnT_s[:, s2 * 384 + k3 * 128:s2 * 384 + (k3 + 1) * 128],
                                    start=(s2 == 0 and k3 == 0),
                                    stop=(s2 == 1 and k3 == 2),
                                    skip_group_check=True)
                        pl.pe_mark()
                        pl.protect(("attnT", s2), "pe", pl.cnt["pe"])
                        if h % 2 == 1 and s2 == 1:
                            vaoc = pl.cnt["pe"]
                            wwoc = pl.need(("woin", m))
                            pl.emit("ac", "copy", waits=(("pe", vaoc),) + wwoc,
                                    incs=(("ac", 1),),
                                    out=woin_s[:, m * 256:(m + 1) * 256],
                                    in_=ps_ao[:, 0:256])
                            pl.protect(("psao",), "ac", pl.cnt["ac"])
                            st[("aoev", w, m)] = pl.cnt["ac"]

                    for X in range(NC_ + 3):
                        if X >= 2 and X - 2 < NC_:
                            emit_B(X - 2)
                        if X >= 3 and X - 3 < NC_:
                            emit_C(X - 3)
                        if X < NC_:
                            emit_A(X)
                    vheads = pl.cnt["pe"]
                    for m in range(DCH):
                        pl.protect(("qraw", par, m), "pe", vheads)
                        pl.protect(("qhi", par, m), "pe", vheads)
                    pl.protect(("vring", slot), "pe", vheads)
                    pl.protect(("vring", slotp), "pe", vheads)
                    st[("scdone", w)] = vheads
                    wr_ = pl.need(("r",))
                    pl.emit("ac", "activation", waits=(("pe", vheads),) + wr_,
                            incs=(("ac", 1),),
                            out=r_s[:, :], in_=ps_sm[32:48, 0:256], func=AF.Ln,
                            bias=cTINY[32:48, 0:1], scale=1.0)
                    pl.emit("ac", "activation", incs=(("ac", 1),),
                            out=r_s[:, :], in_=r_s[:, :], func=AF.Exp,
                            scale=-1.0)
                    vr = pl.cnt["ac"]
                    pl.protect(("psss",), "ac", vr)
                    wr16 = pl.need(("rb16",))
                    pl.emit("ve", "tensor_copy", waits=(("ac", vr),) + wr16,
                            incs=(("ve", 1),),
                            out=rb16_s[:, :], in_=r_s[:, :])
                    vr16 = pl.cnt["ve"]
                    pl.protect(("r",), "ve", vr16)
                    def rb_slot(m):
                        return ps_ao[:, 256:512]

                    def emit_bcast(m):
                        wrb = pl.need(("psrb", 0))
                        if False:
                            wrb = wrb + pl.need(("pssc", 0))
                        pl.emit("pe", "matmul", waits=(("ve", vr16),) + wrb,
                                out=rb_slot(m),
                                lhsT=hselT_sb[0:16, m * 128:(m + 1) * 128],
                                rhs=rb16_s[0:16, 0:256],
                                start=True, stop=True, skip_group_check=True)
                        pl.pe_mark()
                        vrb = pl.cnt["pe"]
                        wrbc = pl.need(("rbc", m % 2))
                        pl.emit("ac", "copy", waits=(("pe", vrb),) + wrbc,
                                incs=(("ac", 1),),
                                out=rbc_s[:, (m % 2) * 256:(m % 2 + 1) * 256],
                                in_=rb_slot(m))
                        vrbe = pl.cnt["ac"]
                        pl.protect(("psrb", 0), "ac", vrbe)
                        pl.emit("ve", "tensor_tensor",
                                waits=(("ac", max(vrbe, st[("aoev", w, m)])),),
                                incs=(("ve", 1),),
                                out=woin_s[:, m * 256:(m + 1) * 256],
                                in0=woin_s[:, m * 256:(m + 1) * 256],
                                in1=rbc_s[:, (m % 2) * 256:(m % 2 + 1) * 256],
                                op=ALU.mult)
                        vwi = pl.cnt["ve"]
                        pl.protect(("rbc", m % 2), "ve", vwi)
                        st[("woin", w, m)] = vwi

                    def emit_woA(m):
                        vwi = st[("woin", w, m)]
                        wdl = pl.need(("psdlA",)) if m == 0 else ()
                        for dm in range(4):
                            pl.emit("pe", "matmul", waits=(("ve", vwi),) + wdl,
                                    out=ps_dl[:, dm * 256:(dm + 1) * 256],
                                    lhsT=wo_sb[:, m * 1024 + dm * 128:m * 1024 + (dm + 1) * 128],
                                    rhs=woin_s[:, m * 256:(m + 1) * 256],
                                    start=(m == 0 and dm % 2 == 0),
                                    stop=(m == DCH - 1), skip_group_check=True)

                    for m in range(DCH + 1):
                        if m < DCH:
                            emit_bcast(m)
                        if m >= 1:
                            emit_woA(m - 1)
                    pl.pe_mark()
                    vwoA = pl.cnt["pe"]
                    wxo = pl.need(("xo",))
                    for dm in range(4):
                        pl.emit("ve", "tensor_tensor",
                                waits=((("pe", vwoA), st[("ldx", w)]) + wxo)
                                if dm == 0 else (),
                                incs=(("ve", 1),),
                                out=xo_s[:, dm * 256:(dm + 1) * 256],
                                in0=ps_dl[:, dm * 256:(dm + 1) * 256],
                                in1=xa_un(par, dm), op=ALU.add)
                    vrA = pl.cnt["ve"]
                    pl.protect(("psdlA",), "ve", vrA)
                    for m in range(DCH):
                        vwi = st[("woin", w, m)]
                        wdl = pl.need(("psdlA",)) if m == 0 else ()
                        for dm in range(4, 8):
                            pl.emit("pe", "matmul", waits=(("ve", vwi),) + wdl,
                                    out=ps_dl[:, (dm - 4) * 256:(dm - 3) * 256],
                                    lhsT=wo_sb[:, m * 1024 + dm * 128:m * 1024 + (dm + 1) * 128],
                                    rhs=woin_s[:, m * 256:(m + 1) * 256],
                                    start=(m == 0 and dm % 2 == 0),
                                    stop=(m == DCH - 1), skip_group_check=True)
                    pl.pe_mark()
                    vwoB = pl.cnt["pe"]
                    for m in range(DCH):
                        pl.protect(("woin", m), "pe", vwoB)
                    for dm in range(4, 8):
                        pl.emit("ve", "tensor_tensor",
                                waits=(("pe", vwoB),) if dm == 4 else (),
                                incs=(("ve", 1),),
                                out=xo_s[:, dm * 256:(dm + 1) * 256],
                                in0=ps_dl[:, (dm - 4) * 256:(dm - 3) * 256],
                                in1=xa_un(par, dm), op=ALU.add)
                    vrB = pl.cnt["ve"]
                    pl.protect(("psdlA",), "ve", vrB)
                    col = PADL + 256 * w
                    dma(pl, "sy", "st",
                        xscr_c[(2 * layer + 1) % 2][:, :, col:col + 256],
                        xa3(xo_s, 0, 256),
                        waits=(("ve", vrB),) + LD_GATE_A)
                    pl.protect(("xo",), "st", pl.cnt["st"])
                    pl.protect(("xa", par), "ve", vrB)
                    pl.protect(("xa", par), "ac", st[("cast", w)])
                    pl.protect(("xb", par), "pe", vwoB)
                    st[("attn_done", w)] = vwoB

                prev = None
                for w in range(a0 - 1, NW):
                    par = (w - (a0 - 1)) % 2
                    plan_load(w, par)
                    plan_cast(w, par)
                    plan_qkproj(w, par, do_q=(w >= a0))
                    ring_war = ()
                    if prev is not None and prev >= a0:
                        plan_attn(prev, (prev - (a0 - 1)) % 2)
                        ring_war = (("pe", st[("scdone", prev)]),)
                    plan_chain(w, do_q=(w >= a0))
                    plan_epi(w, par, do_q=(w >= a0), ring_war=ring_war)
                    plan_vproj(w, par)
                    if w == a0 - 1:
                        pl.protect(("xa", par), "ac", st[("cast", w)])
                        pl.protect(("xa", par), "pe", pl.cnt["pe"])
                        pl.protect(("xb", par), "pe", pl.cnt["pe"])
                    prev = w
                plan_attn(prev, (prev - (a0 - 1)) % 2)
                XSCR_ST[(2 * layer + 1) % 2] = pl.cnt["st"]

                pl.protect(("warena",), "pe", pl.cnt["pe"])
                pl.protect(("warena",), "ve", pl.cnt["ve"])
                pl.protect(("warena",), "po", pl.cnt["po"])
                pl.protect(("w2",), "pe", pl.cnt["pe"])
                pl.protect(("w2",), "ve", pl.cnt["ve"])

            with ExitStack() as sB:
                EB = sB.enter_context
                w1_sb = EB(nc.sbuf_tensor("w1_sb_%d" % layer, [128, DCH * 2 * DIP], BF16))
                ps_ag = EB(nc.psum_tensor("psB_ag_%d" % layer, [128, 1024], F32))
                ps_dl = EB(nc.psum_tensor("psB_dl_%d" % layer, [128, DCH * 256], F32))
                ps_st2 = EB(nc.psum_tensor("psB_st_%d" % layer, [1, 512], F32))
                ps_b2 = EB(nc.psum_tensor("psB_b_%d" % layer, [128, 512], F32))

                LD_GATE_B = (("lx0", pl.cnt["lx0"]), ("lx1", pl.cnt["lx1"]))
                wload = pl.need(("warena",))
                for c in range(DCH):
                    dma(pl, "po", "wW", w1_sb[:, c * 2 * DIP:(c + 1) * 2 * DIP],
                        w1_a[layer, c], waits=wload if c == 0 else ())
                VW1 = pl.cnt["wW"]
                wload2 = pl.need(("w2",))
                for j in range(NPAIR):
                    dma(pl, "po", "wX", w2_sb[:, j * 1024:(j + 1) * 1024],
                        w2_a[layer, j], waits=wload2 if j == 0 else ())
                VW2 = pl.cnt["wX"]

                t0 = 256 * a0
                ntile = (OWN - t0) // TB
                for ti in range(ntile):
                    tok = t0 + ti * TB
                    col = PADL + tok
                    par = ti % 2
                    lxB = "lx%d" % par
                    wx = pl.need(("xa", par))
                    gate = (("st", XSCR_ST[(2 * layer + 1) % 2]),)
                    dma(pl, "sy", lxB, xa3(xa, par * DCH * XT, XT),
                        xscr_c[(2 * layer + 1) % 2][:, :, col - 1:col - 1 + XT],
                        waits=wx + gate)
                    vld = (lxB, pl.cnt[lxB])
                    vedge = 0
                    if tok == 0:
                        for c in range(4, DCH):
                            pl.emit("ve", "tensor_scalar",
                                    waits=(vld,) if c == 4 else (),
                                    incs=(("ve", 1),),
                                    out=xa[:, (par * DCH + c) * XT:(par * DCH + c) * XT + 1],
                                    in0=xa[:, (par * DCH + c) * XT:(par * DCH + c) * XT + 1],
                                    scalar1=edge_sb[:, 0:1], scalar2=None, op0=ALU.mult)
                        vedge = pl.cnt["ve"]
                    wb = pl.need(("xb", par))
                    pl.emit("ac", "copy", waits=wb + (vld, ("ve", vedge)),
                            incs=(("ac", 1),),
                            out=xb[:, par * DCH * XT:(par + 1) * DCH * XT],
                            in_=xa[:, par * DCH * XT:(par + 1) * DCH * XT])
                    vcast = pl.cnt["ac"]

                    def xbsh(kc):
                        off = (par * DCH + kc) * XT + (1 if kc < 4 else 0)
                        return xb[:, off:off + TB]

                    def xash(kc):
                        off = (par * DCH + kc) * XT + (1 if kc < 4 else 0)
                        return xa[:, off:off + TB]

                    wst = pl.need(("psbst",))
                    for c in range(DCH):
                        pl.emit("pe", "matmul",
                                waits=(wst + (("ac", vcast),)) if c == 0 else (),
                                out=ps_st2[0:1, 0:TB], lhsT=onesB[:, :], rhs=xbsh(c),
                                start=(c == 0), stop=(c == DCH - 1),
                                skip_group_check=True)
                    for c in range(DCH):
                        s = c % 2
                        wsq = pl.need(("sq", s))
                        pl.emit("ac", "activation", waits=wsq, incs=(("ac", 1),),
                                out=sq_s[:, s * 256:s * 256 + TB], in_=xbsh(c),
                                func=AF.Square, scale=1.0)
                        vsq = pl.cnt["ac"]
                        pl.emit("pe", "matmul", waits=(("ac", vsq),),
                                out=ps_st2[0:1, TB:2 * TB], lhsT=onesB[:, :],
                                rhs=sq_s[:, s * 256:s * 256 + TB],
                                start=(c == 0), stop=(c == DCH - 1),
                                skip_group_check=True)
                        pl.pe_mark()
                        pl.protect(("sq", s), "pe", pl.cnt["pe"])
                    pl.pe_mark()
                    vstm = pl.cnt["pe"]
                    wmu = pl.need(("musr",))
                    pl.emit("ac", "copy", waits=(("pe", vstm),) + wmu, incs=(("ac", 1),),
                            out=musr_s[0:1, 0:2 * TB], in_=ps_st2[0:1, 0:2 * TB])
                    vmu = pl.cnt["ac"]
                    pl.protect(("psbst",), "ac", vmu)
                    wbb = pl.need(("psbb",))
                    pl.emit("pe", "matmul", waits=(("ac", vmu),) + wbb,
                            out=ps_b2[:, 0:TB], lhsT=onesRF[0:1, :],
                            rhs=musr_s[0:1, 0:TB], start=True, stop=True,
                            skip_group_check=True)
                    pl.emit("pe", "matmul",
                            out=ps_b2[:, TB:2 * TB], lhsT=onesRF[0:1, :],
                            rhs=musr_s[0:1, TB:2 * TB], start=True, stop=True,
                            skip_group_check=True)
                    pl.pe_mark()
                    vbb = pl.cnt["pe"]
                    pl.protect(("musr",), "pe", vbb)
                    pl.emit("ve", "tensor_scalar", waits=(("pe", vbb),), incs=(("ve", 1),),
                            out=mu_s[:, 0:TB], in0=ps_b2[:, 0:TB], scalar1=INVD,
                            scalar2=None, op0=ALU.mult)
                    pl.emit("ve", "tensor_tensor", incs=(("ve", 1),),
                            out=var_s[:, 0:TB], in0=mu_s[:, 0:TB], in1=mu_s[:, 0:TB],
                            op=ALU.mult)
                    pl.emit("ve", "scalar_tensor_tensor", incs=(("ve", 1),),
                            out=var_s[:, 0:TB], in0=ps_b2[:, TB:2 * TB], scalar=INVD,
                            in1=var_s[:, 0:TB], op0=ALU.mult, op1=ALU.subtract)
                    vvv = pl.cnt["ve"]
                    pl.protect(("psbb",), "ve", vvv)
                    pl.emit("ac", "activation", waits=(("ve", vvv),), incs=(("ac", 1),),
                            out=rstd_s[:, 0:TB], in_=var_s[:, 0:TB], func=AF.Ln,
                            bias=cEPS[:, 0:1], scale=1.0)
                    pl.emit("ac", "activation", incs=(("ac", 1),),
                            out=rstd_s[:, 0:TB], in_=rstd_s[:, 0:TB], func=AF.Exp,
                            scale=-0.5)
                    vrs = pl.cnt["ac"]
                    pl.emit("ve", "tensor_tensor", waits=(("ac", vrs),), incs=(("ve", 1),),
                            out=nb_s[:, 0:TB], in0=mu_s[:, 0:TB], in1=rstd_s[:, 0:TB],
                            op=ALU.mult)
                    vnb = pl.cnt["ve"]
                    for c in range(DCH):
                        eng = "ve" if c % 2 == 0 else "po"
                        wu = pl.need(("uB", c % 2))
                        pl.emit(eng, "tensor_tensor", waits=(("ve", vnb),) + wu,
                                incs=((eng, 1),),
                                out=u_s[:, (c % 2) * 256:(c % 2) * 256 + TB],
                                in0=xash(c), in1=rstd_s[:, 0:TB], op=ALU.mult)
                        pl.emit(eng, "tensor_tensor", incs=((eng, 1),),
                                out=u_s[:, (c % 2) * 256:(c % 2) * 256 + TB],
                                in0=u_s[:, (c % 2) * 256:(c % 2) * 256 + TB],
                                in1=nb_s[:, 0:TB], op=ALU.subtract)
                        vu = pl.cnt[eng]
                        wxn = pl.need(("xnfB",))
                        pl.emit("ac", "activation", waits=((eng, vu),) + wxn,
                                incs=(("ac", 1),),
                                out=xnf_s[:, c * 256:c * 256 + TB],
                                in_=u_s[:, (c % 2) * 256:(c % 2) * 256 + TB],
                                func=AF.Identity,
                                bias=lnb_sb[:, layer * DCH + c:layer * DCH + c + 1],
                                scale=lng_sb[:, layer * DCH + c:layer * DCH + c + 1])
                        pl.protect(("uB", c % 2), "ac", pl.cnt["ac"])
                    vxn = pl.cnt["ac"]
                    vgs_arr = {}

                    def emit_w1(j):
                        pj = j % 2
                        wag = pl.need(("psag", pj))
                        for kc in range(DCH):
                            w0 = ()
                            if kc == 0:
                                w0 = wag + (("ac", vxn), ("wW", VW1))
                            pl.emit("pe", "matmul", waits=w0,
                                    out=ps_ag[:, pj * 512:pj * 512 + 256],
                                    lhsT=w1_sb[:, kc * 2 * DIP + 256 * j:kc * 2 * DIP + 256 * j + 128],
                                    rhs=xnf_s[:, kc * 256:kc * 256 + TB],
                                    start=(kc == 0), stop=(kc == DCH - 1),
                                    skip_group_check=True)
                            pl.emit("pe", "matmul",
                                    out=ps_ag[:, pj * 512 + 256:pj * 512 + 512],
                                    lhsT=w1_sb[:, kc * 2 * DIP + 256 * j + 128:kc * 2 * DIP + 256 * j + 256],
                                    rhs=xnf_s[:, kc * 256:kc * 256 + TB],
                                    start=False, stop=(kc == DCH - 1),
                                    skip_group_check=True)
                        pl.pe_mark()
                        vag = pl.cnt["pe"]
                        wg = pl.need(("gg", pj))
                        pl.emit("ac", "activation", waits=(("pe", vag),) + wg,
                                incs=(("ac", 1),),
                                out=gg_s[:, pj * 256:(pj + 1) * 256],
                                in_=ps_ag[:, pj * 512 + 256:pj * 512 + 512],
                                func=AF.Gelu,
                                bias=b1g_sb[:, layer * NPAIR + j:layer * NPAIR + j + 1],
                                scale=1.0)
                        vgg = pl.cnt["ac"]
                        wgs = pl.need(("g", pj))
                        pl.emit("ve", "scalar_tensor_tensor",
                                waits=(("ac", vgg),) + wgs, incs=(("ve", 1),),
                                out=g_s[:, pj * 256:(pj + 1) * 256],
                                in0=ps_ag[:, pj * 512:pj * 512 + 256],
                                scalar=b1a_sb[:, layer * NPAIR + j:layer * NPAIR + j + 1],
                                in1=gg_s[:, pj * 256:(pj + 1) * 256],
                                op0=ALU.add, op1=ALU.mult)
                        vgs_arr[j] = pl.cnt["ve"]
                        pl.protect(("psag", pj), "ve", vgs_arr[j])
                        pl.protect(("gg", pj), "ve", vgs_arr[j])

                    def emit_w2(j):
                        pj = j % 2
                        wdl = pl.need(("psbdl",)) if j == 0 else ()
                        for dm in range(DCH):
                            w0 = (("ve", vgs_arr[j]),)
                            if dm == 0:
                                w0 = w0 + wdl + (("wX", VW2),)
                            pl.emit("pe", "matmul", waits=w0,
                                    out=ps_dl[:, dm * 256:(dm + 1) * 256],
                                    lhsT=w2_sb[:, j * 1024 + dm * 128:j * 1024 + (dm + 1) * 128],
                                    rhs=g_s[:, pj * 256:(pj + 1) * 256],
                                    start=(j == 0 and dm % 2 == 0),
                                    stop=(j == NPAIR - 1),
                                    skip_group_check=True)
                        pl.pe_mark()
                        pl.protect(("g", pj), "pe", pl.cnt["pe"])

                    for j in range(NPAIR + 1):
                        if j < NPAIR:
                            emit_w1(j)
                        if j >= 1:
                            emit_w2(j - 1)
                    vw2m = pl.cnt["pe"]
                    wxo = pl.need(("xo",))
                    for c in range(DCH):
                        pl.emit("ve", "scalar_tensor_tensor",
                                waits=((("pe", vw2m),) + wxo) if c == 0 else (),
                                incs=(("ve", 1),),
                                out=xo_s[:, c * 256:(c + 1) * 256],
                                in0=ps_dl[:, c * 256:(c + 1) * 256],
                                scalar=b2_sb[:, layer * DCH + c:layer * DCH + c + 1],
                                in1=xa[:, (par * DCH + c) * XT + 1:(par * DCH + c) * XT + 1 + TB],
                                op0=ALU.add, op1=ALU.add)
                    vres = pl.cnt["ve"]
                    pl.protect(("psbdl",), "ve", vres)
                    pl.protect(("xa", par), "ve", vres)
                    pl.protect(("xb", par), "pe", vw2m)
                    pl.protect(("xnfB",), "pe", vw2m)
                    dma(pl, "sy", "st",
                        xscr_c[(2 * layer + 2) % 2][:, :, col:col + TB],
                        xa3(xo_s, 0, 256),
                        waits=(("ve", vres),) + LD_GATE_B)
                    pl.protect(("xo",), "st", pl.cnt["st"])
                XSCR_ST[(2 * layer + 2) % 2] = pl.cnt["st"]
                pl.protect(("warena",), "pe", pl.cnt["pe"])
                pl.protect(("w2",), "pe", pl.cnt["pe"])

        xf = (2 * depth_emit) % 2
        p13s = top.enter_context(nc.sbuf_tensor("mjn13", [128, DCH * 256], F32))
        mjn_s = p13s
        pl.protect(("mjnF",), "pe", pl.cnt["pe"])
        pl.protect(("mjnF",), "ve", pl.cnt["ve"])
        pl.protect(("mjnF",), "po", pl.cnt["po"])
        pl.protect(("mjnF",), "ac", pl.cnt["ac"])
        for t in range(OWN // 256):
            col = PADL + t * 256
            par = t % 2
            lx13 = "lx%d" % par
            wx = pl.need(("xa", par))
            gate = (("st", XSCR_ST[xf]),)
            dma(pl, "sy", lx13, xa3(xa, par * DCH * XT, XT)[:, :, 0:256],
                xscr_c[xf][:, :, col:col + 256], waits=wx + gate)
            vld1 = pl.cnt[lx13]
            wm = pl.need(("mjnF",))
            ga = pl.need(("acs",))
            dma(pl, "sy", "lm", xa3(mjn_s, 0, 256),
                acs_c[:, :, t * 256:(t + 1) * 256], waits=wm + ga)
            vld2 = pl.cnt["lm"]
            wj = pl.need(("xo",))
            dma(pl, "sy", "lj", xa3(xo_s, 0, 256),
                jext_c[:, :, col:col + 256], waits=wj)
            vld3 = pl.cnt["lj"]
            for c in range(DCH):
                pl.emit("ve", "tensor_tensor",
                        waits=((lx13, vld1), ("lm", vld2), ("lj", vld3)) if c == 0 else (),
                        incs=(("ve", 1),),
                        out=mjn_s[:, c * 256:(c + 1) * 256],
                        in0=mjn_s[:, c * 256:(c + 1) * 256],
                        in1=xa[:, (par * DCH + c) * XT:(par * DCH + c) * XT + 256],
                        op=ALU.add)
                pl.emit("ve", "tensor_tensor", incs=(("ve", 1),),
                        out=xo_s[:, c * 256:(c + 1) * 256],
                        in0=xo_s[:, c * 256:(c + 1) * 256],
                        in1=mjn_s[:, c * 256:(c + 1) * 256], op=ALU.add)
            vsum = pl.cnt["ve"]
            pl.protect(("mjnF",), "ve", vsum)
            pl.protect(("xa", par), "ve", vsum)
            dma(pl, "sy", "st", outT_c[:, :, t * 256:(t + 1) * 256],
                xa3(xo_s, 0, 256), waits=(("ve", vsum),))
            pl.protect(("xo",), "st", pl.cnt["st"])

        pl.emit("sy", "wait_only",
                waits=(("st", pl.cnt["st"]), ("pe", pl.cnt["pe"]),
                       ("ve", pl.cnt["ve"]), ("ac", pl.cnt["ac"]),
                       ("po", pl.cnt["po"]), ("wW", pl.cnt["wW"]),
                       ("wT", pl.cnt["wT"]), ("wX", pl.cnt["wX"]), ("lx0", pl.cnt["lx0"]),
                       ("lx1", pl.cnt["lx1"]), ("lm", pl.cnt["lm"]),
                       ("lj", pl.cnt["lj"])))

        with nc.Block() as block:
            def runner(engname):
                def run(eng):
                    for (e, waits, meth, args, kwargs, incs) in pl.ops:
                        if e != engname:
                            continue
                        for (s, v) in waits:
                            eng.wait_ge(sems[s], v)
                        if meth == "wait_only":
                            continue
                        if meth == "engine_nop" and engname in ("pe", "sy", "ac"):
                            meth = "nop"
                        ins = getattr(eng, meth)(*args, **kwargs)
                        for (s, n) in incs:
                            ins.then_inc(sems[s], n)
                return run
            block.sync(runner("sy"))
            block.tensor(runner("pe"))
            block.vector(runner("ve"))
            block.scalar(runner("ac"))
            block.gpsimd(runner("po"))

    return nc


def _mk_tables(gpos):
    n = NSEQ
    half = np.arange(0, DH, 2, dtype=np.float32)
    inv_freq = (1.0 / (10000.0 ** (half / np.float32(DH)))).astype(np.float32)
    t = gpos.astype(np.float32)
    freqs = (t[:, None] * inv_freq[None, :]).astype(np.float32)
    base = ((half + 0.4 * DH) / (1.4 * DH)).astype(np.float32)
    power = ((t - n // 2) / np.float32(WIN // 2)).astype(np.float32)
    zeta = (base[None, :] ** power[:, None]).astype(np.float32)
    c = np.cos(freqs).astype(np.float32)
    s = np.sin(freqs).astype(np.float32)
    inv = (1.0 / zeta).astype(np.float32)
    L = len(gpos)
    out = np.zeros((6, 128, L), np.float32)
    rows = {
        0: ((8.0 * zeta * c).T, (8.0 * zeta * c).T),
        1: ((8.0 * zeta * s).T, (8.0 * zeta * s).T),
        2: ((inv * c).T, (inv * c).T),
        3: ((-inv * s).T, (inv * s).T),
        4: ((inv * s).T, (inv * s).T),
        5: ((inv * c).T, (-inv * c).T),
    }
    for ti in range(6):
        a, b = rows[ti]
        for g in range(2):
            o = g * 64
            out[ti, o:o + 32] = a
            out[ti, o + 32:o + 64] = b
    return out


def _mk_maskT(first_window, s2=0):
    m = np.zeros((128, 384), np.float32)
    jr = np.arange(128)[:, None]
    ic = np.arange(128)[None, :]
    for blk in range(3):
        d = ic - 128 * (blk - 2) - jr
        v = (d >= 0) & (d <= 256)
        if first_window:
            jglob = 128 * (blk + s2 - 2) + jr
            v = v & (jglob >= 0)
        m[:, blk * 128:(blk + 1) * 128] = v.astype(np.float32)
    return m


_CACHE = {}


def kernel(L0_attn_J, L0_attn_W_Q, L0_attn_W_K, L0_attn_gamma, L0_attn_beta,
           L0_ffn_Wqkv, L0_ffn_q_scale, L0_ffn_k_scale, L0_ffn_Wo, L0_ffn_ln_g,
           L0_ffn_ln_b, L0_ffn_W1, L0_ffn_b1, L0_ffn_W2, L0_ffn_b2):
    f32 = np.float32
    J = np.asarray(L0_attn_J, f32)
    WQ = np.asarray(L0_attn_W_Q, f32)
    WK = np.asarray(L0_attn_W_K, f32)
    gam = np.asarray(L0_attn_gamma, f32)
    bet = np.asarray(L0_attn_beta, f32)
    Wqkv = np.asarray(L0_ffn_Wqkv, f32)
    qsc = np.asarray(L0_ffn_q_scale, f32)
    ksc = np.asarray(L0_ffn_k_scale, f32)
    Wo = np.asarray(L0_ffn_Wo, f32)
    ln_g = np.asarray(L0_ffn_ln_g, f32)
    ln_b = np.asarray(L0_ffn_ln_b, f32)
    W1 = np.asarray(L0_ffn_W1, f32)
    b1 = np.asarray(L0_ffn_b1, f32)
    W2 = np.asarray(L0_ffn_W2, f32)
    b2 = np.asarray(L0_ffn_b2, f32)

    wqkv_h = np.ascontiguousarray(Wqkv.reshape(DEPTH, DCH, 128, 3072))
    wo_h = np.ascontiguousarray(Wo.reshape(DEPTH, DCH, 128, 1024))
    w1p = np.zeros((DEPTH, 1024, 2 * DIP), f32)
    w1p[:, :, 0:DI] = W1[:, :, 0:DI]
    w1p[:, :, DIP:DIP + DI] = W1[:, :, DI:2 * DI]
    w1q = np.zeros_like(w1p)
    for j in range(NPAIR):
        w1q[:, :, 256 * j:256 * j + 128] = w1p[:, :, 128 * j:128 * j + 128]
        w1q[:, :, 256 * j + 128:256 * j + 256] = w1p[:, :, DIP + 128 * j:DIP + 128 * j + 128]
    w1_h = np.ascontiguousarray(w1q.reshape(DEPTH, DCH, 128, 2 * DIP))
    w2p = np.zeros((DEPTH, DIP, 1024), f32)
    w2p[:, 0:DI] = W2
    w2_h = np.ascontiguousarray(w2p.reshape(DEPTH, NPAIR, 128, 1024))
    wqT_h = np.ascontiguousarray(WQ.T.reshape(DCH, 128, DK))
    wkT_h = np.ascontiguousarray(WK.T.reshape(DCH, 128, DK))
    qs_h = np.ascontiguousarray(np.concatenate([qsc, qsc], 1).T)
    ks_h = np.ascontiguousarray(np.concatenate([ksc, ksc], 1).T)
    lng_h = np.zeros((128, DEPTH * DCH), f32)
    lnb_h = np.zeros((128, DEPTH * DCH), f32)
    b2_h = np.zeros((128, DEPTH * DCH), f32)
    for l in range(DEPTH):
        for c in range(DCH):
            lng_h[:, l * DCH + c] = ln_g[l, 128 * c:128 * (c + 1)]
            lnb_h[:, l * DCH + c] = ln_b[l, 128 * c:128 * (c + 1)]
            b2_h[:, l * DCH + c] = b2[l, 128 * c:128 * (c + 1)]
    b1p = np.zeros((DEPTH, 2 * DIP), f32)
    b1p[:, 0:DI] = b1[:, 0:DI]
    b1p[:, DIP:DIP + DI] = b1[:, DI:2 * DI]
    b1a_h = np.zeros((128, DEPTH * NPAIR), f32)
    b1g_h = np.zeros((128, DEPTH * NPAIR), f32)
    for l in range(DEPTH):
        for j in range(NPAIR):
            b1a_h[:, l * NPAIR + j] = b1p[l, 128 * j:128 * j + 128]
            b1g_h[:, l * NPAIR + j] = b1p[l, DIP + 128 * j:DIP + 128 * j + 128]
    gam_h = np.ascontiguousarray(gam.reshape(DCH, 128).T)
    bet_h = np.ascontiguousarray(bet.reshape(DCH, 128).T)
    identF_h = np.eye(128, dtype=f32)
    onesC_h = np.ones((128, 1), f32)
    onesR_h = np.ones((1, 128), f32)
    hselT_h = np.zeros((16, DCH * 128), f32)
    hselB_h = np.zeros((128, DCH * 16), f32)
    for c in range(DCH):
        for p in range(128):
            hh = 2 * c + (p // 64)
            hselT_h[hh, c * 128 + p] = 1.0
            hselB_h[p, c * 16 + hh] = 1.0
    ehsel_h = np.zeros((128, 256), f32)
    for hh in range(16):
        ehsel_h[:, hh * 16 + hh] = 1.0
    mask_n = _mk_maskT(False)
    mask_f = np.concatenate([_mk_maskT(True, 0), _mk_maskT(True, 1)], 1)
    mask_n2 = np.concatenate([mask_n, mask_n], 1)

    shared = dict(wqkv=wqkv_h, wo=wo_h, w1=w1_h, w2=w2_h, wqT=wqT_h, wkT=wkT_h,
                  qsv=qs_h, ksv=ks_h, lng=lng_h, lnb=lnb_h, b2v=b2_h,
                  b1a=b1a_h, b1g=b1g_h, gamv=gam_h, betv=bet_h,
                  identF=identF_h, onesC=onesC_h, onesR=onesR_h,
                  hselT=hselT_h, hselB=hselB_h, ehsel=ehsel_h, maskre=mask_n)

    in_maps = []
    for core in range(8):
        b, h = core // 2, core % 2
        Jt = np.ascontiguousarray(J[b].T)
        je = np.zeros((D, XW), f32)
        g0 = OWN * h - PADL
        lo, hi = max(0, g0), OWN * h + OWN
        je[:, lo - g0:hi - g0] = Jt[:, lo:hi]
        jo = np.ascontiguousarray(Jt[:, OWN * (1 - h):OWN * (2 - h)])
        gpos = np.arange(g0, OWN * h + OWN)
        tabs_h = _mk_tables(gpos)
        edge_h = np.full((128, 1), 0.0 if h == 0 else 1.0, f32)
        m = dict(shared)
        m.update(jext=je, joth=jo, tabs=tabs_h, edgev=edge_h,
                 maskw0=(mask_f if h == 0 else mask_n2))
        in_maps.append(m)

    if "nc" not in _CACHE:
        _CACHE["nc"] = build_program(DEPTH)
    nc = _CACHE["nc"]
    res = run_bass_kernel_spmd(nc, in_maps, core_ids=list(range(8)))
    out = np.zeros((NB, NSEQ, D), f32)
    for core in range(8):
        b, h = core // 2, core % 2
        out[b, OWN * h:OWN * (h + 1), :] = res.results[core]["outT"].T
    return out
